# version 06 (final)
# baseline (speedup 1.0000x reference)
import math
import numpy as np
import concourse.bass as bass
import concourse.mybir as mybir
from concourse.bass_utils import run_bass_kernel_spmd

F32 = mybir.dt.float32
I32 = mybir.dt.int32
U32 = mybir.dt.uint32
AF = mybir.ActivationFunctionType
ALU = mybir.AluOpType
AX = mybir.AxisListType.X

D = 2048
H = 16
DK = 128
NG = 8
GG = 16
C = 128
OCN = D // 128
KCN = D // 128
FF = 1408
FFC = FF // 128
FFN_E = 6
GCLAMP = -20.0
ENGS = ("sync", "tensor", "vector", "scalar", "gpsimd")


class _Op:
    __slots__ = ("eng", "fn", "inc", "need_inc", "deps", "cum", "tag",
                 "dma", "slot")

    def __init__(self, eng, fn, inc, tag, dma):
        self.eng, self.fn, self.inc, self.tag = eng, fn, inc, tag
        self.dma = dma
        self.need_inc = dma
        self.deps = []
        self.cum = 0
        self.slot = 0


NSLOT = {"sync": 8, "gpsimd": 4}


class Sched:

    def __init__(self):
        self.ops = []
        self.res = {}
        self.barriers = []

    def add(self, eng, fn, reads=(), writes=(), dma=False, inc=None, tag=""):
        assert eng in ENGS, eng
        idx = len(self.ops)
        op = _Op(eng, fn, inc if inc else (16 if dma else 1), tag,
                 dma or (inc is not None and inc >= 16))
        self.ops.append(op)
        dep = set()
        for r in reads:
            e = self.res.setdefault(r, [None, []])
            if e[0] is not None:
                dep.add(e[0])
            e[1].append(idx)
        for w in writes:
            e = self.res.setdefault(w, [None, []])
            if e[0] is not None:
                dep.add(e[0])
            dep.update(e[1])
            self.res[w] = [idx, []]
        dep.discard(idx)
        op.deps = sorted(dep)
        return idx

    def barrier(self):
        self.barriers.append(len(self.ops))

    def plan(self):
        ops = self.ops
        for op in ops:
            for j in op.deps:
                ops[j].need_inc = True
        bset = sorted(set(self.barriers))
        for b in bset:
            last = {}
            for i in range(b):
                last[ops[i].eng] = i
            for i in last.values():
                ops[i].need_inc = True
        def key(op):
            return (op.eng, op.slot) if op.dma else op.eng
        slot_rr = {e: 0 for e in NSLOT}
        cum = {}
        prev_on_slot = {}
        gates = {}
        for idx, op in enumerate(ops):
            if op.dma:
                k = slot_rr[op.eng]
                slot_rr[op.eng] = (k + 1) % NSLOT[op.eng]
                op.slot = k
                sk = (op.eng, k)
                if sk in prev_on_slot:
                    gates[idx] = (sk, cum.get(sk, 0))
                prev_on_slot[sk] = idx
                if op.need_inc or True:
                    cum[sk] = cum.get(sk, 0) + op.inc
                op.cum = cum[sk]
            else:
                sk = op.eng
                if op.need_inc:
                    cum[sk] = cum.get(sk, 0) + op.inc
                op.cum = cum.get(sk, 0)
        bar_t = []
        c2 = {}
        oi = 0
        for b in bset:
            while oi < b:
                o = ops[oi]
                if o.need_inc or o.dma:
                    sk = key(o)
                    c2[sk] = o.cum
                oi += 1
            bar_t.append(dict(c2))
        all_keys = set()
        for op in ops:
            all_keys.add(key(op))
        waited = {e: {} for e in ENGS}
        streams = {e: [] for e in ENGS}

        def wait_to(e, sk, v):
            if v > waited[e].get(sk, 0):
                waited[e][sk] = v
                streams[e].append(("w", sk, v))
        bi = 0
        for idx, op in enumerate(ops):
            while bi < len(bset) and bset[bi] == idx:
                for e in ENGS:
                    for sk, v in bar_t[bi].items():
                        wait_to(e, sk, v)
                bi += 1
            if idx in gates:
                sk, v = gates[idx]
                wait_to(op.eng, sk, v)
            need = {}
            for j in op.deps:
                pj = ops[j]
                sk = key(pj)
                need[sk] = max(need.get(sk, 0), pj.cum)
            for sk, v in sorted(need.items(), key=str):
                wait_to(op.eng, sk, v)
            streams[op.eng].append(("o", op))
        fin = {}
        for op in ops:
            if op.need_inc or op.dma:
                fin[key(op)] = op.cum
        for e in ENGS:
            for sk, v in fin.items():
                wait_to(e, sk, v)
        return streams


def _host_consts():
    s = np.arange(C)[:, None]
    i = np.arange(C)[None, :]
    masks = np.zeros((7, 128, 128), np.uint32)
    masks[0] = (i // 4 == s // 4) & (i > s)
    masks[1] = (i // 4 == s // 4) & (i >= s)
    for li, m in enumerate((8, 16, 32, 64, 128)):
        masks[2 + li] = (s // m == i // m) & (s % m < m // 2) \
            & (i % m >= m // 2)
    cstv = np.zeros((128, 16), np.float32)
    cstv[:, 0] = 1e-5
    cstv[:, 1] = 1e-6
    cstv[:, 2] = -0.5
    cstv[:, 3] = math.log(1.0 / math.sqrt(DK))
    cstv[:, 4] = 1.0 / DK
    cstv[:, 5] = 1.0 / D
    cstv[:, 6] = 1.0
    cstv[:, 7] = 0.0
    cstv[:, 8] = -1.0
    lt = (np.arange(C)[:, None] <= np.arange(C)[None, :]).astype(np.float32)
    sel = np.zeros((16, 128, 128), np.float32)
    for j in range(16):
        for ch in range(128):
            sel[j, 8 * j + ch // GG, ch] = 1.0
    return masks, cstv, lt, sel


def stage_inputs(inputs, T=4096, B=4):
    Tc = T // 2
    NCH = T // C
    x = np.asarray(inputs["L0_attn_hidden_states"], np.float32)
    g = lambda k: np.asarray(inputs[k], np.float32)
    wq, wk, wv = g("L0_attn_Wq"), g("L0_attn_Wk"), g("L0_attn_Wv")
    cq, ck, cv = g("L0_attn_conv_q"), g("L0_attn_conv_k"), g("L0_attn_conv_v")
    wf1, wf2, wbm = g("L0_attn_Wf1"), g("L0_attn_Wf2"), g("L0_attn_Wb")
    alog, dtb = g("L0_attn_A_log"), g("L0_attn_dt_bias")
    wg, bg = g("L0_attn_Wg"), g("L0_attn_bg")
    normw, wo = g("L0_attn_norm_w"), g("L0_attn_Wo")
    gw = g("L0_ffn_gate_W")
    w1, b1 = g("L0_ffn_W1"), g("L0_ffn_b1")
    w2, b2 = g("L0_ffn_W2"), g("L0_ffn_b2")
    masks, cstv, lt, sel = _host_consts()
    ident = np.eye(128, dtype=np.float32)

    in_maps = []
    for core in range(2 * B):
        b, p = core // 2, core % 2
        hperm = np.arange(16)
        cD = np.arange(D)
        c128 = np.arange(128)
        xfm = np.zeros((2, D // 128, 128, Tc + 3), np.float32)
        for r in range(2):
            xs = x[b, r * Tc:(r + 1) * Tc, :]
            xh = (x[b, r * Tc - 3:r * Tc, :] if r == 1
                  else np.zeros((3, D), np.float32))
            xfm[r] = np.concatenate([xh, xs], 0).T.reshape(D // 128, 128,
                                                           Tc + 3)
        xfm = np.ascontiguousarray(xfm)
        m = {
            "xfm": xfm,
            "wq": np.ascontiguousarray(wq[:, cD]),
            "wk": np.ascontiguousarray(wk[:, cD]),
            "wv": np.ascontiguousarray(wv[:, cD]),
            "wg": np.ascontiguousarray(wg[:, cD]),
            "wo": np.ascontiguousarray(wo[cD, :]),
            "wb": np.ascontiguousarray(wbm[:, hperm]),
            "wf1": np.ascontiguousarray(wf1),
            "wf2": np.ascontiguousarray(wf2[:, c128]),
            "convq": np.ascontiguousarray(cq[cD, :]),
            "convk": np.ascontiguousarray(ck[cD, :]),
            "convv": np.ascontiguousarray(cv[cD, :]),
            "alog": np.ascontiguousarray(
                np.repeat(alog[hperm], NG).reshape(128, 1)),
            "dtb": np.ascontiguousarray(
                dtb.reshape(H, NG)[hperm].reshape(128, 1)),
            "bgv": np.ascontiguousarray(bg[cD].reshape(OCN, 128).T),
            "normw": np.ascontiguousarray(normw.reshape(128, 1)),
            "gwt": np.ascontiguousarray(gw.T),
            "w1": w1, "w2": w2,
            "b1s": np.ascontiguousarray(
                b1.reshape(FFN_E, FFC, 128).transpose(0, 2, 1)),
            "b2s": np.ascontiguousarray(
                b2.reshape(FFN_E, OCN, 128).transpose(0, 2, 1)),
            "ident": ident, "ltones": lt,
            "sel": sel, "cstv": cstv, "masks7": masks,
            "ones_p": np.ones((128, 1), np.float32),
            "ones_r": np.ones((1, 128), np.float32),
            "iota6": np.arange(8, dtype=np.float32).reshape(1, 8),
        }
        in_maps.append(m)
    return in_maps


def build(T=4096, ncores=8, dbg=False,
          phases=("ln1", "proj", "exch1", "scan", "p3", "ffn")):
    Tc = T // 2
    MW = Tc + 4
    NCH = T // C
    TCW = min(512, Tc)
    NTC = Tc // TCW
    nc = bass.Bass("TRN2", target_bir_lowering=False,
                   disable_frame_to_traceback=True)

    def dram(name, shape, dtype=F32, kind="ExternalInput"):
        return nc.dram_tensor(name, list(shape), dtype, kind=kind).ap()

    xfm_d = dram("xfm", [2, OCN, 128, Tc + 3])
    w_d = {k: dram(k, [D, D]) for k in ("wq", "wk", "wv", "wg", "wo")}
    wb_d = dram("wb", [D, 16])
    wf1_d = dram("wf1", [D, 128])
    wf2_d = dram("wf2", [128, 128])
    conv_d = {"q": dram("convq", [D, 4]), "k": dram("convk", [D, 4]),
              "v": dram("convv", [D, 4])}
    alog_d = dram("alog", [128, 1])
    dtb_d = dram("dtb", [128, 1])
    bgv_d = dram("bgv", [128, OCN])
    normw_d = dram("normw", [128, 1])
    gwt_d = dram("gwt", [D, 6])
    w1_d = dram("w1", [FFN_E, D, FF])
    w2_d = dram("w2", [FFN_E, FF, D])
    b1s_d = dram("b1s", [FFN_E, 128, FFC])
    b2s_d = dram("b2s", [FFN_E, 128, OCN])
    ident_d = dram("ident", [128, 128])
    lt_d = dram("ltones", [128, 128])
    sel_d = dram("sel", [16, 128, 128])
    cstv_d = dram("cstv", [128, 16])
    masks_d = dram("masks7", [7, 128, 128], U32)
    onesp_d = dram("ones_p", [128, 1])
    onesr_d = dram("ones_r", [1, 128])
    iota6_d = dram("iota6", [1, 8])
    XGR = 3 * D + 128 + 16
    KND = "ExternalOutput" if dbg else "Internal"
    xgx_d = dram("xgx_s", [2 * XGR, Tc], kind=KND)
    CANR = 3 * 2048 + 128 + 16
    can_d = dram("can_s", [CANR, T], kind="Internal")
    sg_d = dram("sg_s", [D, T], kind=KND)
    ofm_d = dram("ofm_s", [2048, T], kind=KND)
    w1c_d = dram("w1c_s", [2 * D, FF], kind="Internal")
    w2c_d = dram("w2c_s", [2 * FF, D], kind="Internal")
    x1_d = dram("x1_s", [D, T], kind=KND)
    ln2_d = dram("ln2_s", [D, T], kind="Internal")
    f0_d = dram("f0_s", [D, T], kind="Internal")
    out_d = dram("out", [D, T], kind="ExternalOutput")

    BFW = max(OCN * MW, 33000)
    bigfm = nc.alloc_sbuf_tensor("bigfm", [128, BFW], F32)
    wcol = nc.alloc_sbuf_tensor("wcol", [128, 2, KCN, 128], F32)
    scr2 = nc.alloc_sbuf_tensor("scr2", [128, max(2 * MW, FFC * TCW)], F32)
    sqt = nc.alloc_sbuf_tensor("sqt", [128, 2, TCW], F32)
    rows = nc.alloc_sbuf_tensor("rows", [1, 3, max(TCW, 128)], F32)
    stg = nc.alloc_sbuf_tensor("stg", [128, 6, max(TCW, 128)], F32)
    convw = nc.alloc_sbuf_tensor("convw", [128, KCN, 4], F32)
    smv = nc.alloc_sbuf_tensor("smv", [128, 8], F32)
    bgs = nc.alloc_sbuf_tensor("bgs", [128, OCN], F32)
    csts = nc.alloc_sbuf_tensor("csts", [128, 16], F32)
    identt = nc.alloc_sbuf_tensor("identt", [128, 128], F32)
    ltt = nc.alloc_sbuf_tensor("ltt", [128, 128], F32)
    selt = nc.alloc_sbuf_tensor("selt", [128, 16, 128], F32)
    maskt = nc.alloc_sbuf_tensor("maskt", [128, 7, 128], U32)
    onespt = nc.alloc_sbuf_tensor("onespt", [128, 1], F32)
    onesrt = nc.alloc_sbuf_tensor("onesrt", [1, 128], F32)
    iota6t = nc.alloc_sbuf_tensor("iota6t", [1, 8], F32)
    tabs = nc.alloc_sbuf_tensor("tabs", [1, 16], I32)
    b1c = nc.alloc_sbuf_tensor("b1c", [128, 2, FFC], F32)
    b2c = nc.alloc_sbuf_tensor("b2c", [128, 2, OCN], F32)
    gsm = nc.alloc_sbuf_tensor("gsm", [128, 40], F32)
    gwts = nc.alloc_sbuf_tensor("gwts", [128, OCN, 6], F32)
    grow = nc.alloc_sbuf_tensor("grow", [1, 32], F32)
    growu = nc.alloc_sbuf_tensor("growu", [1, 8], U32)
    gidx = nc.alloc_sbuf_tensor("gidx", [1, 16], I32)

    psg = [nc.psum_tensor(f"pb{i}", [128, 512], F32) for i in range(8)]
    pb = [g.__enter__() for g in psg]

    S = Sched()
    main_tc = [(3 + j * TCW, TCW) for j in range(NTC)]

    def bfa(oc, c0, w):
        return bigfm.ap()[:, oc * MW + c0: oc * MW + c0 + w]

    def bfr(oc, c0):
        return f"M:{oc}:{c0}"

    def load_consts():
        def mk(dst, src, res):
            S.add("sync", lambda e, d=dst, s=src: e.dma_start(out=d, in_=s),
                  writes=[res], dma=True, tag="cst")
        mk(csts.ap()[:], cstv_d[:], "csts")
        mk(identt.ap()[:], ident_d[:], "identt")
        mk(ltt.ap()[:], lt_d[:], "ltt")
        mk(selt.ap()[:], sel_d[:].transpose([1, 0, 2]), "selt")
        mk(onespt.ap()[:], onesp_d[:], "onespt")
        mk(onesrt.ap()[:], onesr_d[:], "onesrt")
        mk(iota6t.ap()[:], iota6_d[:], "iota6")
        mk(maskt.ap()[:], masks_d[:].transpose([1, 0, 2]), "maskt")
        mk(smv.ap()[:, 0:1], alog_d[:], "smv_a")
        mk(smv.ap()[:, 1:2], dtb_d[:], "smv_d")
        mk(smv.ap()[:, 2:3], normw_d[:], "smv_n")
        mk(bgs.ap()[:], bgv_d[:], "bgs")
        S.add("scalar", lambda e: e.activation(
            out=smv.ap()[:, 3:4], in_=smv.ap()[:, 0:1], func=AF.Exp),
            reads=["smv_a"], writes=["smv_e"], tag="nega")
        S.add("vector", lambda e: e.tensor_scalar_mul(
            out=smv.ap()[:, 3:4], in0=smv.ap()[:, 3:4], scalar1=-1.0),
            reads=["smv_e"], writes=["smv_e"], tag="nega2")

    def ln_pass(tag, halo):
        chunks = list(main_tc) + ([(0, 3)] if halo else [])
        for (c0, w) in chunks:
            t = f"{tag}{c0}"
            for kc in range(KCN):
                sq = sqt.ap()[:, kc % 2, 0:w]
                S.add("scalar", lambda e, kc=kc, sq=sq, c0=c0, w=w:
                      e.activation(out=sq, in_=bfa(kc, c0, w),
                                   func=AF.Square),
                      reads=[bfr(kc, c0)], writes=[f"sq{kc % 2}"],
                      tag=f"{t}q{kc}")
                S.add("tensor", lambda e, kc=kc, c0=c0, w=w: e.matmul(
                    pb[6][0:1, 0:w], onespt.ap()[:], bfa(kc, c0, w),
                    start=(kc == 0), stop=(kc == KCN - 1)),
                    reads=[bfr(kc, c0), "onespt"], writes=["pb6"],
                    tag=f"{t}s{kc}")
                S.add("tensor", lambda e, kc=kc, sq=sq, w=w: e.matmul(
                    pb[7][0:1, 0:w], onespt.ap()[:], sq,
                    start=(kc == 0), stop=(kc == KCN - 1)),
                    reads=[f"sq{kc % 2}", "onespt"], writes=["pb7"],
                    tag=f"{t}z{kc}")
            mean = rows.ap()[0:1, 0, 0:w]
            msq = rows.ap()[0:1, 1, 0:w]
            var = rows.ap()[0:1, 1, 0:w]
            rstd = rows.ap()[0:1, 1, 0:w]
            mr = rows.ap()[0:1, 2, 0:w]
            S.add("vector", lambda e, mean=mean, w=w: e.tensor_scalar_mul(
                out=mean, in0=pb[6][0:1, 0:w], scalar1=1.0 / D),
                reads=["pb6"], writes=["r0"], tag=f"{t}m")
            S.add("vector", lambda e, msq=msq, w=w: e.tensor_scalar_mul(
                out=msq, in0=pb[7][0:1, 0:w], scalar1=1.0 / D),
                reads=["pb7"], writes=["r1"], tag=f"{t}q")
            S.add("vector", lambda e, mean=mean, mr=mr: e.tensor_tensor(
                out=mr, in0=mean, in1=mean, op=ALU.mult),
                reads=["r0"], writes=["r2"], tag=f"{t}v1")
            S.add("vector", lambda e, msq=msq, mr=mr, var=var:
                  e.tensor_tensor(out=var, in0=msq, in1=mr, op=ALU.subtract),
                  reads=["r1", "r2"], writes=["r1"], tag=f"{t}v2")
            S.add("scalar", lambda e, var=var, rstd=rstd: e.activation(
                out=rstd, in_=var, func=AF.Ln, bias=csts.ap()[0:1, 0:1]),
                reads=["r1", "csts"], writes=["r1"], tag=f"{t}l")
            S.add("scalar", lambda e, rstd=rstd: e.activation(
                out=rstd, in_=rstd, func=AF.Exp, scale=csts.ap()[0:1, 2:3]),
                reads=["r1", "csts"], writes=["r1"], tag=f"{t}e")
            S.add("vector", lambda e, mean=mean, rstd=rstd, mr=mr:
                  e.tensor_tensor(out=mr, in0=mean, in1=rstd, op=ALU.mult),
                  reads=["r0", "r1", "r2"], writes=["r2"], tag=f"{t}mr")
            S.add("tensor", lambda e, rstd=rstd, w=w: e.matmul(
                pb[5][:, 0:w], onesrt.ap()[:], rstd, start=True, stop=True),
                reads=["r1", "onesrt"], writes=["pb5"], tag=f"{t}b1")
            S.add("tensor", lambda e, mr=mr, w=w: e.matmul(
                pb[4][:, 0:w], onesrt.ap()[:], mr, start=True, stop=True),
                reads=["r2", "onesrt"], writes=["pb4"], tag=f"{t}b2")
            for oc in range(OCN):
                st = stg.ap()[:, 4 + oc % 2, 0:w]
                S.add("vector", lambda e, oc=oc, c0=c0, w=w, st=st:
                      e.tensor_tensor(out=st, in0=bfa(oc, c0, w),
                                      in1=pb[5][:, 0:w], op=ALU.mult),
                      reads=[bfr(oc, c0), "pb5"], writes=[f"st{4 + oc % 2}"],
                      tag=f"{t}a{oc}")
                S.add("vector", lambda e, oc=oc, c0=c0, w=w, st=st:
                      e.tensor_tensor(out=bfa(oc, c0, w), in0=st,
                                      in1=pb[4][:, 0:w], op=ALU.subtract),
                      reads=[f"st{4 + oc % 2}", "pb4"], writes=[bfr(oc, c0)],
                      tag=f"{t}o{oc}")

    def phase_ln1(r):
        for oc in range(OCN):
            S.add("sync", lambda e, oc=oc, r=r: e.dma_start(
                out=bfa(oc, 0, Tc + 3), in_=xfm_d[r, oc]),
                writes=[bfr(oc, c0) for (c0, _) in main_tc + [(0, 3)]],
                dma=True, tag=f"ldx{r}_{oc}")
        ln_pass(f"L1{r}", True)

    def proj(wd, ocn, mdim, epilogue, halo, tag):
        chunks = (list(main_tc) + [(0, 3)]) if halo else list(main_tc)
        for oc in range(ocn):
            dbi = oc % 2
            wres = f"wc{dbi}"
            S.add("sync", lambda e, oc=oc, dbi=dbi: e.dma_start(
                out=wcol.ap()[:, dbi, :, 0:mdim],
                in_=wd[:, oc * mdim:(oc + 1) * mdim].rearrange(
                    "(kc p) m -> p kc m", p=128)),
                reads=[wres], writes=[wres], dma=True, tag=f"{tag}w{oc}")
            pieces = []
            for j, (c0, w) in enumerate(chunks):
                bank = (4 + (oc % 2) if w == 3 else j % 4)
                pres = f"pb{bank}"

                def mk(dbi=dbi, c0=c0, w=w, bank=bank, mdim=mdim):
                    def f(e):
                        r = None
                        for kc in range(KCN):
                            r = e.matmul(
                                pb[bank][0:mdim, 0:w],
                                wcol.ap()[:, dbi, kc, 0:mdim],
                                bfa(kc, c0, w),
                                start=(kc == 0), stop=(kc == KCN - 1))
                        return r
                    return f
                S.add("tensor", mk(),
                      reads=[wres] + [bfr(kc, c0) for kc in range(KCN)],
                      writes=[pres], tag=f"{tag}mm{oc}_{j}")
                pieces.append((j, c0, w, pb[bank][0:mdim, 0:w], pres))
            epilogue(oc, pieces)

    def conv_epi(dst_d, key, tag):
        S.add("sync", lambda e: e.dma_start(
            out=convw.ap()[:], in_=conv_d[key][:].rearrange(
                "(kc p) m -> p kc m", p=128)),
            writes=[f"cw{tag}"], dma=True, tag=f"{tag}cw")

        def epi(oc, pieces):
            pi = oc % 2
            pres = f"pr{pi}"
            pr = scr2.ap()[:, pi * MW:pi * MW + MW]
            for (j, c0, w, ps, psres) in pieces:
                S.add("vector", lambda e, c0=c0, w=w, ps=ps, pr=pr:
                      e.tensor_copy(out=pr[:, c0:c0 + w], in_=ps),
                      reads=[psres], writes=[pres], tag=f"{tag}e{oc}_{j}")
            for j, (c0, w) in enumerate(main_tc):
                use = stg.ap()[:, 2 + j % 2, 0:w]
                ur = f"st{2 + j % 2}"
                S.add("vector", lambda e, c0=c0, w=w, use=use, oc=oc, pr=pr:
                      e.tensor_scalar(out=use, in0=pr[:, c0:c0 + w],
                                      scalar1=convw.ap()[:, oc, 3:4],
                                      scalar2=None, op0=ALU.mult),
                      reads=[pres, f"cw{tag}"], writes=[ur],
                      tag=f"{tag}c3_{oc}_{j}")
                for dj in (1, 2, 3):
                    S.add("vector",
                          lambda e, c0=c0, w=w, use=use, oc=oc, dj=dj, pr=pr:
                          e.scalar_tensor_tensor(
                              out=use, in0=pr[:, c0 - dj:c0 - dj + w],
                              scalar=convw.ap()[:, oc, 3 - dj:4 - dj],
                              in1=use, op0=ALU.mult, op1=ALU.add),
                          reads=[pres, f"cw{tag}", ur], writes=[ur],
                          tag=f"{tag}c{dj}_{oc}_{j}")
                S.add("scalar", lambda e, use=use: e.activation(
                    out=use, in_=use, func=AF.Silu),
                    reads=[ur], writes=[ur], tag=f"{tag}s{oc}_{j}")
                S.add("sync", lambda e, oc=oc, c0=c0, w=w, use=use:
                      e.dma_start(out=dst_d[oc * 128:(oc + 1) * 128,
                                            c0 - 3:c0 - 3 + w], in_=use),
                      reads=[ur], writes=[f"{tag}o{oc}_{j}"], dma=True,
                      tag=f"{tag}d{oc}_{j}")
        return epi

    def phase_gbg(r):
        def epi_g1(oc, pieces):
            for (j, c0, w, ps, psres) in pieces:
                S.add("vector", lambda e, c0=c0, w=w, ps=ps:
                      e.tensor_copy(out=scr2.ap()[:, c0:c0 + w], in_=ps),
                      reads=[psres], writes=["pr0"], tag=f"g1e{r}_{j}")
        proj(wf1_d, 1, 128, epi_g1, False, f"g1{r}")
        S.add("sync", lambda e: e.dma_start(
            out=wcol.ap()[:, 1, 0, :], in_=wf2_d[:]),
            reads=["wc1"], writes=["wc1"], dma=True, tag=f"g2w{r}")
        for j, (c0, w) in enumerate(main_tc):
            bank = 4 + (j % 2)
            pres = f"pb{bank}"
            S.add("tensor", lambda e, c0=c0, w=w, bank=bank: e.matmul(
                pb[bank][:, 0:w], wcol.ap()[:, 1, 0, :],
                scr2.ap()[:, c0:c0 + w], start=True, stop=True),
                reads=["wc1", "pr0"], writes=[pres], tag=f"g2m{r}_{j}")
            zb = stg.ap()[:, 0, 0:w]
            ab = stg.ap()[:, 1, 0:w]
            S.add("vector", lambda e, w=w, bank=bank, zb=zb: e.tensor_scalar(
                out=zb, in0=pb[bank][:, 0:w], scalar1=smv.ap()[:, 1:2],
                scalar2=None, op0=ALU.add),
                reads=[pres, "smv_d"], writes=["st0"], tag=f"gz{r}_{j}")
            S.add("scalar", lambda e, zb=zb, ab=ab: e.activation(
                out=ab, in_=zb, func=AF.Abs),
                reads=["st0"], writes=["st1"], tag=f"ga{r}_{j}")
            S.add("scalar", lambda e, ab=ab: e.activation(
                out=ab, in_=ab, func=AF.Exp, scale=csts.ap()[:, 8:9]),
                reads=["st1", "csts"], writes=["st1"], tag=f"gb{r}_{j}")
            S.add("scalar", lambda e, ab=ab: e.activation(
                out=ab, in_=ab, func=AF.Ln, bias=csts.ap()[:, 6:7]),
                reads=["st1", "csts"], writes=["st1"], tag=f"gc{r}_{j}")
            S.add("vector", lambda e, zb=zb: e.tensor_scalar_max(
                out=zb, in0=zb, scalar1=0.0),
                reads=["st0"], writes=["st0"], tag=f"gd{r}_{j}")
            S.add("vector", lambda e, zb=zb, ab=ab: e.tensor_tensor(
                out=zb, in0=zb, in1=ab, op=ALU.add),
                reads=["st0", "st1"], writes=["st0"], tag=f"ge{r}_{j}")
            S.add("vector", lambda e, zb=zb: e.tensor_scalar(
                out=zb, in0=zb, scalar1=smv.ap()[:, 3:4], scalar2=GCLAMP,
                op0=ALU.mult, op1=ALU.max),
                reads=["st0", "smv_e"], writes=["st0"], tag=f"gf{r}_{j}")
            S.add("sync", lambda e, c0=c0, w=w, zb=zb, r=r: e.dma_start(
                out=xgx_d[r * XGR + 3 * D:r * XGR + 3 * D + 128,
                          c0 - 3:c0 - 3 + w], in_=zb),
                reads=["st0"], writes=[f"ggo{r}_{j}"], dma=True,
                tag=f"gg{r}_{j}")

        def epi_b(oc, pieces):
            for (j, c0, w, ps, psres) in pieces:
                S.add("scalar", lambda e, w=w, ps=ps: e.activation(
                    out=stg.ap()[0:16, 0, 0:w], in_=ps, func=AF.Sigmoid),
                    reads=[psres], writes=["st0"], tag=f"bs{r}_{j}")
                S.add("sync", lambda e, c0=c0, w=w, r=r: e.dma_start(
                    out=xgx_d[r * XGR + 3 * D + 128:r * XGR + XGR,
                              c0 - 3:c0 - 3 + w],
                    in_=stg.ap()[0:16, 0, 0:w]),
                    reads=["st0"], writes=[f"bbo{r}_{j}"], dma=True,
                    tag=f"bd{r}_{j}")
        proj(wb_d, 1, 16, epi_b, False, f"b{r}")

    def phase_gate(r):
        def epi(oc, pieces):
            for (j, c0, w, ps, psres) in pieces:
                st = stg.ap()[:, 2 + j % 2, 0:w]
                ur = f"st{2 + j % 2}"
                S.add("scalar", lambda e, oc=oc, w=w, ps=ps, st=st:
                      e.activation(out=st, in_=ps, func=AF.Silu,
                                   bias=bgs.ap()[:, oc:oc + 1]),
                      reads=[psres, "bgs"], writes=[ur], tag=f"sg{r}_{oc}_{j}")
                S.add("sync", lambda e, oc=oc, c0=c0, w=w, st=st, r=r:
                      e.dma_start(out=sg_d[oc * 128:(oc + 1) * 128,
                                           r * Tc + c0 - 3:
                                           r * Tc + c0 - 3 + w], in_=st),
                      reads=[ur], writes=[f"sgo{r}_{oc}_{j}"], dma=True,
                      tag=f"sgd{r}_{oc}_{j}")
        proj(w_d["wg"], OCN, 128, epi, False, f"sg{r}")

    def phase_repack():
        rds = ([f"{tg}{r}o{oc}_{j}" for tg in ("q", "k", "v")
                for r in range(2) for oc in range(OCN) for j in range(NTC)]
               + [f"ggo{r}_{j}" for r in range(2) for j in range(NTC)]
               + [f"bbo{r}_{j}" for r in range(2) for j in range(NTC)])
        for r in range(2):
            S.add("sync", lambda e, r=r: e.dma_start(
                out=bass.AP(can_d.tensor, r * Tc,
                            [[2048 * T, 3], [T, 2048], [1, Tc]]),
                in_=bass.AP(xgx_d.tensor, r * XGR * Tc,
                            [[D * Tc, 3], [Tc, 2048], [1, Tc]])),
                reads=rds + ["canq"], writes=["canq"], dma=True,
                tag=f"rpq{r}")
            S.add("sync", lambda e, r=r: e.dma_start(
                out=bass.AP(can_d.tensor, 3 * 2048 * T + r * Tc,
                            [[T, 128], [1, Tc]]),
                in_=bass.AP(xgx_d.tensor, (r * XGR + 3 * D) * Tc,
                            [[Tc, 128], [1, Tc]])),
                reads=rds + ["cang"], writes=["cang"], dma=True,
                tag=f"rpg{r}")
            S.add("sync", lambda e, r=r: e.dma_start(
                out=bass.AP(can_d.tensor, (3 * 2048 + 128) * T + r * Tc,
                            [[T, 16], [1, Tc]]),
                in_=bass.AP(xgx_d.tensor, (r * XGR + 3 * D + 128) * Tc,
                            [[Tc, 16], [1, Tc]])),
                reads=rds + ["canb"], writes=["canb"], dma=True,
                tag=f"rpb{r}")

    def phase_scan():
        def cv(off, n):
            return bigfm.ap()[:, off:off + n], off + n
        off = 0
        qall, off = cv(off, 2048)
        kall, off = cv(off, 2048)
        vall, off = cv(off, 2048)
        gst, off = cv(off, 128)
        bst, off = cv(off, 128)
        Gx, off = cv(off, 2048)
        Eall, off = cv(off, 2048)
        EHt, off = cv(off, 2048)
        kE, off = cv(off, 2048)
        qE, off = cv(off, 2048)
        kEh, off = cv(off, 2048)
        krow, off = cv(off, 2048)
        kcolb, off = cv(off, 2048)
        qcolb, off = cv(off, 2048)
        Sst, off = cv(off, 2048)
        Cts, off = cv(off, 2048)
        Ats, off = cv(off, 2048)
        A1, off = cv(off, 128)
        A1T, off = cv(off, 128)
        Apow, off = cv(off, 6 * 128)
        Tbuf, off = cv(off, 128)
        tsol, off = cv(off, 128)
        vtok, off = cv(off, 128)
        otok, off = cv(off, 128)
        gtokc, off = cv(off, 128)
        gfmc, off = cv(off, 128)
        dcol, off = cv(off, 16)
        rqk, off = cv(off, 32)
        assert off <= BFW, off
        NS = 16

        S.add("vector", lambda e: e.memset(Sst[:], 0.0),
              writes=[f"S{j}" for j in range(NS)], tag="sini")

        levels = [(4, 0, 4), (8, 4, 8), (16, 8, 16), (32, 16, 32),
                  (64, 32, 64), (128, 64, 128)]

        for ch in range(NCH):
            t = f"c{ch}_"
            rc = (ch * C) // Tc
            tau = ch * C - rc * Tc
            qar, kar, var_ = "qa", "ka", "va"
            gsr, bsr = "gs", "bs"

            def ld_qkv(e, rc=rc, tau=tau):
                o = []
                for bi, dst in enumerate((qall, kall, vall)):
                    src = bass.AP(xgx_d.tensor,
                                  (rc * XGR + bi * D) * Tc + tau,
                                  [[Tc, 128], [128 * Tc, NS], [1, 128]])
                    o.append(e.dma_start(
                        out=dst.rearrange("p (s u) -> p s u", s=NS),
                        in_=src))
                return o
            S.add("sync", ld_qkv,
                  reads=[f"{tg}{rc}o{oc}_{jj}" for tg in ("q", "k", "v")
                         for oc in range(OCN) for jj in range(NTC)]
                  + [qar, kar, var_],
                  writes=[qar, kar, var_], inc=48, tag=f"{t}lq")

            def ld_gb(e, rc=rc, tau=tau):
                i1 = e.dma_start(out=gst[0:128, 0:128], in_=bass.AP(
                    xgx_d.tensor, (rc * XGR + 3 * D) * Tc + tau,
                    [[Tc, 128], [1, 128]]))
                i2 = e.dma_start(out=bst[0:16, 0:128], in_=bass.AP(
                    xgx_d.tensor, (rc * XGR + 3 * D + 128) * Tc + tau,
                    [[Tc, 16], [1, 128]]))
                return [i1, i2]
            S.add("sync", ld_gb,
                  reads=[f"ggo{rc}_{jj}" for jj in range(NTC)]
                  + [f"bbo{rc}_{jj}" for jj in range(NTC)] + [gsr, bsr],
                  writes=[gsr, bsr], inc=32, tag=f"{t}lg")

            S.add("tensor", lambda e: e.transpose(
                pb[7][:, 0:128], gst[0:128, 0:128], identt.ap()[:]),
                reads=[gsr, "identt"], writes=["pb7"], tag=f"{t}gT")
            S.add("vector", lambda e: e.tensor_copy(
                out=gtokc[:, 0:128], in_=pb[7][:, 0:128]),
                reads=["pb7"], writes=["gtk"], tag=f"{t}gTe")
            S.add("tensor", lambda e: e.transpose(
                pb[7][:, 128:144], bst[0:16, 0:128],
                identt.ap()[0:16, 0:16]),
                reads=[bsr, "identt", "gtk"], writes=["pb7"], tag=f"{t}bT")
            S.add("vector", lambda e: e.tensor_copy(
                out=dcol[:, 0:16], in_=pb[7][:, 128:144]),
                reads=["pb7"], writes=["btk"], tag=f"{t}bTe")
            S.add("tensor", lambda e: e.matmul(
                pb[6][:, 0:128], gtokc[:, 0:128], ltt.ap()[:],
                start=True, stop=True),
                reads=["gtk", "ltt"], writes=["pb6"], tag=f"{t}cu")
            S.add("vector", lambda e: e.tensor_copy(
                out=gfmc[:, :], in_=pb[6][:, 0:128]),
                reads=["pb6"], writes=["gfm"], tag=f"{t}cue")
            for j in range(NS):
                bk = 6 + (j % 2)
                S.add("tensor", lambda e, j=j, bk=bk: e.matmul(
                    pb[bk][:, 128:256], selt.ap()[:, j, :], gfmc[:, :],
                    start=True, stop=True),
                    reads=["gfm", "selt"], writes=[f"pb{bk}"],
                    tag=f"{t}xp{j}")
                S.add("vector", lambda e, j=j, bk=bk: e.tensor_copy(
                    out=Gx[:, j * 128:(j + 1) * 128],
                    in_=pb[bk][:, 128:256]),
                    reads=[f"pb{bk}"], writes=["Gx"], tag=f"{t}xe{j}")

            S.add("scalar", lambda e: e.activation(
                out=Eall[:], in_=Gx[:], func=AF.Exp),
                reads=["Gx"], writes=["Ea"], tag=f"{t}E")

            def ehf(e):
                gl = bass.AP(Gx.tensor, Gx.offset + 127,
                             [list(Gx.ap[0]), [128, NS], [0, 128]])
                return e.tensor_tensor(
                    out=EHt.rearrange("p (s u) -> p s u", s=NS),
                    in0=gl, in1=Gx.rearrange("p (s u) -> p s u", s=NS),
                    op=ALU.subtract)
            S.add("vector", ehf, reads=["Gx"], writes=["EH"], tag=f"{t}h1")
            S.add("scalar", lambda e: e.activation(
                out=EHt[:], in_=EHt[:], func=AF.Exp),
                reads=["EH"], writes=["EH"], tag=f"{t}h2")
            S.add("gpsimd", lambda e: e.tensor_tensor(
                out=kE[:], in0=kall[:], in1=Eall[:], op=ALU.mult),
                reads=[kar, "Ea"], writes=["kE"], tag=f"{t}kE")
            S.add("gpsimd", lambda e: e.tensor_tensor(
                out=qE[:], in0=qall[:], in1=Eall[:], op=ALU.mult),
                reads=[qar, "Ea"], writes=["qE"], tag=f"{t}qE")
            S.add("gpsimd", lambda e: e.tensor_tensor(
                out=kEh[:], in0=kall[:], in1=EHt[:], op=ALU.mult),
                reads=[kar, "EH"], writes=["kEh"], tag=f"{t}kh")

            for (src, rsn, col) in ((qall, qar, 0), (kall, kar, 1)):
                S.add("scalar", lambda e, src=src: e.activation(
                    out=krow[:], in_=src[:], func=AF.Square),
                    reads=[rsn, "kr0"], writes=["kr0"], tag=f"{t}sq{col}")
                for j in range(NS):
                    S.add("tensor", lambda e, j=j, col=col: e.matmul(
                        pb[7][:, 256 + col * NS + j:257 + col * NS + j],
                        krow[:, j * 128:(j + 1) * 128], onespt.ap()[:],
                        start=True, stop=True),
                        reads=["kr0", "onespt"], writes=["pb7"],
                        tag=f"{t}ss{col}_{j}")
                S.add("scalar", lambda e, col=col: e.activation(
                    out=rqk[:, col * NS:(col + 1) * NS],
                    in_=pb[7][:, 256 + col * NS:256 + (col + 1) * NS],
                    func=AF.Ln, bias=csts.ap()[:, 1:2]),
                    reads=["pb7", "csts"], writes=["rq"], tag=f"{t}r{col}")
                S.add("scalar", lambda e, col=col: e.activation(
                    out=rqk[:, col * NS:(col + 1) * NS],
                    in_=rqk[:, col * NS:(col + 1) * NS], func=AF.Exp,
                    scale=csts.ap()[:, 2:3],
                    bias=(csts.ap()[:, 3:4] if col == 0
                          else csts.ap()[:, 7:8])),
                    reads=["rq", "csts"], writes=["rq"], tag=f"{t}x{col}")

            for li, (m, refoff, blk) in enumerate(levels):
                kr = krow
                kc_ = kcolb
                qc_ = qcolb
                rr = ["kr0", "kc0", "qc0"]

                def mkD(e, refoff=refoff, blk=blk):
                    gref = bass.AP(
                        Gx.tensor, Gx.offset + refoff,
                        [list(Gx.ap[0]), [blk, 2048 // blk], [0, blk]])
                    return e.tensor_tensor(
                        out=krow.rearrange("p (b u) -> p b u", u=blk),
                        in0=Gx.rearrange("p (b u) -> p b u", u=blk),
                        in1=gref, op=ALU.subtract)
                S.add("vector", mkD, reads=["Gx", "kr0"], writes=["kr0"],
                      tag=f"{t}D{li}")
                S.add("scalar", lambda e: e.activation(
                    out=qcolb[:], in_=krow[:], func=AF.Exp,
                    scale=csts.ap()[:, 8:9]),
                    reads=["kr0", "qc0", "csts"], writes=["qc0"],
                    tag=f"{t}Y{li}")
                S.add("vector", lambda e: e.tensor_scalar_min(
                    out=krow[:], in0=krow[:], scalar1=0.0),
                    reads=["kr0", "qc0"], writes=["kr0"], tag=f"{t}Dc{li}")
                S.add("scalar", lambda e: e.activation(
                    out=kcolb[:], in_=krow[:], func=AF.Exp),
                    reads=["kr0", "kc0"], writes=["kc0"], tag=f"{t}X{li}")
                S.add("vector", lambda e: e.tensor_tensor(
                    out=krow[:], in0=kall[:], in1=qcolb[:], op=ALU.mult),
                    reads=[kar, "qc0", "kr0"], writes=["kr0"],
                    tag=f"{t}kr{li}")
                S.add("vector", lambda e: e.tensor_tensor(
                    out=qcolb[:], in0=qall[:], in1=kcolb[:], op=ALU.mult),
                    reads=[qar, "kc0", "qc0"], writes=["qc0"],
                    tag=f"{t}qc{li}")
                S.add("vector", lambda e: e.tensor_tensor(
                    out=kcolb[:], in0=kall[:], in1=kcolb[:], op=ALU.mult),
                    reads=[kar, "kc0"], writes=["kc0"], tag=f"{t}kc{li}")
                for j in range(NS):
                    bk = j // 2
                    qk = (j % 2) * 2
                    S.add("tensor", lambda e, j=j, bk=bk, qk=qk: e.matmul(
                        pb[bk][:, qk * 128:(qk + 1) * 128],
                        krow[:, j * 128:(j + 1) * 128],
                        kcolb[:, j * 128:(j + 1) * 128],
                        start=True, stop=True),
                        reads=["kr0", "kc0"], writes=[f"pb{bk}"],
                        tag=f"{t}K{li}_{j}")
                    S.add("tensor", lambda e, j=j, bk=bk, qk=qk: e.matmul(
                        pb[bk][:, (qk + 1) * 128:(qk + 2) * 128],
                        krow[:, j * 128:(j + 1) * 128],
                        qcolb[:, j * 128:(j + 1) * 128],
                        start=True, stop=True),
                        reads=["kr0", "qc0"], writes=[f"pb{bk}"],
                        tag=f"{t}A{li}_{j}")
                for j in range(NS):
                    bk = j // 2
                    qk = (j % 2) * 2
                    ctj = Cts[:, j * 128:(j + 1) * 128]
                    atj = Ats[:, j * 128:(j + 1) * 128]
                    if li == 0:
                        S.add("vector", lambda e, ctj=ctj: e.memset(
                            ctj, 0.0), reads=[f"Ct{j}"], writes=[f"Ct{j}"],
                            tag=f"{t}z1{j}")
                        S.add("vector", lambda e, atj=atj: e.memset(
                            atj, 0.0), reads=[f"At{j}"], writes=[f"At{j}"],
                            tag=f"{t}z2{j}")
                        mi_c, mi_a = 0, 1
                    else:
                        mi_c = mi_a = li + 1
                    S.add("vector", lambda e, j=j, bk=bk, qk=qk, ctj=ctj,
                          mi_c=mi_c: e.copy_predicated(
                              out=ctj, mask=maskt.ap()[:, mi_c, :],
                              data=pb[bk][:, qk * 128:(qk + 1) * 128]),
                          reads=[f"pb{bk}", "maskt", f"Ct{j}"],
                          writes=[f"Ct{j}"], tag=f"{t}mc{li}_{j}")
                    S.add("vector", lambda e, j=j, bk=bk, qk=qk, atj=atj,
                          mi_a=mi_a: e.copy_predicated(
                              out=atj, mask=maskt.ap()[:, mi_a, :],
                              data=pb[bk][:, (qk + 1) * 128:(qk + 2) * 128]),
                          reads=[f"pb{bk}", "maskt", f"At{j}"],
                          writes=[f"At{j}"], tag=f"{t}ma{li}_{j}")

            for j in range(NS):
                P = j % 2
                pA = 4 + 2 * P
                pB = 5 + 2 * P
                rA, rB = f"pb{pA}", f"pb{pB}"
                ctj = Cts[:, j * 128:(j + 1) * 128]
                atj = Ats[:, j * 128:(j + 1) * 128]
                sj = Sst[:, j * 128:(j + 1) * 128]
                sjr = f"S{j}"
                bcol = dcol[:, j:j + 1]
                rkc = rqk[:, NS + j:NS + j + 1]
                rqc = rqk[:, j:j + 1]
                tt = f"{t}{j}_"
                btl = gsm.ap()[:, j:j + 1]
                brk = gsm.ap()[:, NS + j:NS + j + 1]
                S.add("vector", lambda e, rkc=rkc, btl=btl:
                      e.tensor_tensor(out=btl, in0=rkc, in1=rkc,
                                      op=ALU.mult),
                      reads=["btk", "rq"], writes=[f"bt{j}"], tag=tt + "b1")
                S.add("vector", lambda e, bcol=bcol, btl=btl:
                      e.tensor_tensor(out=btl, in0=btl, in1=bcol,
                                      op=ALU.mult),
                      reads=[f"bt{j}", "btk"], writes=[f"bt{j}"],
                      tag=tt + "b2")
                S.add("vector", lambda e, bcol=bcol, rkc=rkc, brk=brk:
                      e.tensor_tensor(out=brk, in0=bcol, in1=rkc,
                                      op=ALU.mult),
                      reads=["btk", "rq"], writes=[f"bk{j}"], tag=tt + "b3")
                S.add("tensor", lambda e, j=j, pA=pA: e.transpose(
                    pb[pA][:, 0:128], vall[:, j * 128:(j + 1) * 128],
                    identt.ap()[:]),
                    reads=[var_, "identt"], writes=[rA], tag=tt + "vT")
                S.add("scalar", lambda e, pA=pA: e.copy(
                    out=vtok[:], in_=pb[pA][:, 0:128]),
                    reads=[rA, "vtk"], writes=["vtk"], tag=tt + "ve")
                S.add("tensor", lambda e, j=j, sj=sj, pB=pB: e.matmul(
                    pb[pB][:, 0:128], kE[:, j * 128:(j + 1) * 128], sj,
                    start=True, stop=True),
                    reads=["kE", sjr], writes=[rB], tag=tt + "ws")
                S.add("vector", lambda e, btl=btl, pB=pB: e.tensor_scalar(
                    out=stg.ap()[:, 0, 0:128], in0=pb[pB][:, 0:128],
                    scalar1=btl, scalar2=None, op0=ALU.mult),
                    reads=[rB, f"bt{j}", "st0"], writes=["st0"],
                    tag=tt + "w2")
                S.add("vector", lambda e, brk=brk: e.scalar_tensor_tensor(
                    out=tsol[:], in0=vtok[:], scalar=brk,
                    in1=stg.ap()[:, 0, 0:128], op0=ALU.mult,
                    op1=ALU.subtract),
                    reads=["vtk", f"bk{j}", "st0", "tso"], writes=["tso"],
                    tag=tt + "R")
                S.add("tensor", lambda e, ctj=ctj, pA=pA: e.transpose(
                    pb[pA][:, 0:128], ctj, identt.ap()[:]),
                    reads=[f"Ct{j}", "identt", "vtk"], writes=[rA],
                    tag=tt + "cT")
                S.add("vector", lambda e, btl=btl, pA=pA: e.tensor_scalar(
                    out=A1T[:], in0=pb[pA][:, 0:128], scalar1=btl,
                    scalar2=None, op0=ALU.mult),
                    reads=[rA, f"bt{j}", "A1T"], writes=["A1T"],
                    tag=tt + "a1t")
                S.add("tensor", lambda e, pA=pA: e.transpose(
                    pb[pA][:, 128:256], A1T[:], identt.ap()[:]),
                    reads=["A1T", "identt"], writes=[rA], tag=tt + "a1T2")
                S.add("scalar", lambda e, pA=pA: e.copy(
                    out=A1[:], in_=pb[pA][:, 128:256]),
                    reads=[rA, "A1"], writes=["A1"], tag=tt + "a1e")
                curA, curAr = A1, "A1"
                curT, curTr = A1T, "A1T"
                for st_i in range(6):
                    ap_ = Apow[:, st_i * 128:(st_i + 1) * 128]
                    apr = f"Ap{st_i}"
                    S.add("tensor", lambda e, curT=curT, curA=curA, pB=pB:
                          e.matmul(pb[pB][:, 128:256], curT, curA,
                                   start=True, stop=True),
                          reads=[curTr, curAr], writes=[rB],
                          tag=tt + f"p{st_i}")
                    S.add("scalar", lambda e, ap_=ap_, pB=pB: e.copy(
                        out=ap_, in_=pb[pB][:, 128:256]),
                        reads=[rB, apr], writes=[apr], tag=tt + f"pe{st_i}")
                    if st_i < 5:
                        S.add("tensor", lambda e, ap_=ap_, pB=pB: e.transpose(
                            pb[pB][:, 256:384], ap_, identt.ap()[:]),
                            reads=[apr, "identt"], writes=[rB],
                            tag=tt + f"pt{st_i}")
                        S.add("scalar", lambda e, pB=pB: e.copy(
                            out=Tbuf[:], in_=pb[pB][:, 256:384]),
                            reads=[rB, "Tb"], writes=["Tb"],
                            tag=tt + f"pte{st_i}")
                        curT, curTr = Tbuf, "Tb"
                    curA, curAr = ap_, apr
                for st_i in range(5, -1, -1):
                    ap_ = Apow[:, st_i * 128:(st_i + 1) * 128]
                    apr = f"Ap{st_i}"
                    S.add("tensor", lambda e, ap_=ap_, pA=pA: e.matmul(
                        pb[pA][:, 256:384], ap_, tsol[:],
                        start=True, stop=True),
                        reads=[apr, "tso"], writes=[rA], tag=tt + f"f{st_i}")
                    S.add("vector", lambda e, pA=pA: e.tensor_tensor(
                        out=tsol[:], in0=tsol[:], in1=pb[pA][:, 256:384],
                        op=ALU.add),
                        reads=["tso", rA], writes=["tso"],
                        tag=tt + f"fa{st_i}")
                S.add("tensor", lambda e, pA=pA: e.matmul(
                    pb[pA][:, 256:384], A1[:], tsol[:],
                    start=True, stop=True),
                    reads=["A1", "tso"], writes=[rA], tag=tt + "fN")
                S.add("vector", lambda e, pA=pA: e.tensor_tensor(
                    out=tsol[:], in0=tsol[:], in1=pb[pA][:, 256:384],
                    op=ALU.subtract),
                    reads=["tso", rA], writes=["tso"], tag=tt + "fS")
                S.add("tensor", lambda e, atj=atj, pB=pB: e.matmul(
                    pb[pB][:, 0:128], atj, tsol[:], start=True, stop=False),
                    reads=[f"At{j}", "tso"], writes=[rB], tag=tt + "o1")
                S.add("tensor", lambda e, j=j, sj=sj, pB=pB: e.matmul(
                    pb[pB][:, 0:128], qE[:, j * 128:(j + 1) * 128], sj,
                    start=False, stop=True),
                    reads=["qE", sjr], writes=[rB], tag=tt + "o2")
                S.add("vector", lambda e, rqc=rqc, pB=pB: e.tensor_scalar(
                    out=otok[:], in0=pb[pB][:, 0:128], scalar1=rqc,
                    scalar2=None, op0=ALU.mult),
                    reads=[rB, "rq", "otk"], writes=["otk"], tag=tt + "o3")
                S.add("tensor", lambda e, pB=pB: e.transpose(
                    pb[pB][:, 128:256], otok[:], identt.ap()[:]),
                    reads=["otk", "identt"], writes=[rB], tag=tt + "oT")
                S.add("scalar", lambda e, pB=pB: e.copy(
                    out=stg.ap()[:, 1, 0:128], in_=pb[pB][:, 128:256]),
                    reads=[rB, "st1"], writes=["st1"], tag=tt + "oe")
                S.add("sync", lambda e, j=j, ch=ch: e.dma_start(
                    out=ofm_d[j * 128:(j + 1) * 128, ch * C:(ch + 1) * C],
                    in_=stg.ap()[:, 1, 0:128]),
                    reads=["st1"], writes=[f"ofmo{j}_{ch}"], dma=True,
                    tag=tt + "od")
                S.add("tensor", lambda e, j=j, pB=pB: e.transpose(
                    pb[pB][:, 256:384], kEh[:, j * 128:(j + 1) * 128],
                    identt.ap()[:]),
                    reads=["kEh", "identt"], writes=[rB], tag=tt + "hT")
                S.add("scalar", lambda e, pB=pB: e.copy(
                    out=Tbuf[:], in_=pb[pB][:, 256:384]),
                    reads=[rB, "Tb"], writes=["Tb"], tag=tt + "he")
                S.add("tensor", lambda e, pB=pB: e.matmul(
                    pb[pB][:, 384:512], Tbuf[:], tsol[:],
                    start=True, stop=True),
                    reads=["Tb", "tso"], writes=[rB], tag=tt + "su")
                ecc = bass.AP(Eall.tensor, Eall.offset + j * 128 + 127,
                              [list(Eall.ap[0]), [1, 1]])
                S.add("vector", lambda e, sj=sj, ecc=ecc, pB=pB:
                      e.scalar_tensor_tensor(
                          out=sj, in0=sj, scalar=ecc,
                          in1=pb[pB][:, 384:512], op0=ALU.mult, op1=ALU.add),
                      reads=[sjr, "Ea", rB], writes=[sjr], tag=tt + "sf")

    def phase_p3(r):
        for tc_i, (c0, w) in enumerate(main_tc):
            for hs in range(16):
                orr = f"st{hs % 2}"
                ot = stg.ap()[:, hs % 2, 0:w]

                def ldo(e, hs=hs, c0=c0, w=w, ot=ot, r=r):
                    return e.dma_start(out=ot, in_=ofm_d[
                        hs * 128:(hs + 1) * 128,
                        r * Tc + c0 - 3:r * Tc + c0 - 3 + w])
                S.add("sync", ldo,
                      reads=[f"ofmo{hs}_{cc}" for cc in range(NCH)] + [orr],
                      writes=[orr], dma=True, tag=f"p3l{r}_{hs}_{tc_i}")
                sgr = f"st{2 + hs % 2}"
                sgt_ = stg.ap()[:, 2 + hs % 2, 0:w]
                S.add("sync", lambda e, hs=hs, c0=c0, w=w, sgt_=sgt_, r=r:
                      e.dma_start(out=sgt_, in_=sg_d[
                          hs * 128:(hs + 1) * 128,
                          r * Tc + c0 - 3:r * Tc + c0 - 3 + w]),
                      reads=[f"sgo{r}_{hs}_{jj}" for jj in range(NTC)]
                      + [sgr], writes=[sgr], dma=True,
                      tag=f"p3s{r}_{hs}_{tc_i}")
                yt = stg.ap()[:, 4 + hs % 2, 0:w]
                yr = f"st{4 + hs % 2}"
                S.add("vector", lambda e, ot=ot, sgt_=sgt_, yt=yt:
                      e.tensor_tensor(out=yt, in0=ot, in1=sgt_,
                                      op=ALU.mult),
                      reads=[orr, sgr, yr], writes=[yr],
                      tag=f"p3y{r}_{hs}_{tc_i}")
                sq = sqt.ap()[:, hs % 2, 0:w]
                S.add("scalar", lambda e, yt=yt, sq=sq: e.activation(
                    out=sq, in_=yt, func=AF.Square),
                    reads=[yr, f"sq{hs % 2}"], writes=[f"sq{hs % 2}"],
                    tag=f"p3q{r}_{hs}_{tc_i}")
                S.add("tensor", lambda e, sq=sq, w=w: e.matmul(
                    pb[6][0:1, 0:w], onespt.ap()[:], sq,
                    start=True, stop=True),
                    reads=[f"sq{hs % 2}", "onespt"], writes=["pb6"],
                    tag=f"p3m{r}_{hs}_{tc_i}")
                rr_ = rows.ap()[0:1, 2, 0:w]
                S.add("scalar", lambda e, rr_=rr_, w=w: e.activation(
                    out=rr_, in_=pb[6][0:1, 0:w], func=AF.Ln,
                    scale=csts.ap()[0:1, 4:5], bias=csts.ap()[0:1, 0:1]),
                    reads=["pb6", "csts", "r2"], writes=["r2"],
                    tag=f"p3r{r}_{hs}_{tc_i}")
                S.add("scalar", lambda e, rr_=rr_: e.activation(
                    out=rr_, in_=rr_, func=AF.Exp,
                    scale=csts.ap()[0:1, 2:3]),
                    reads=["r2", "csts"], writes=["r2"],
                    tag=f"p3e{r}_{hs}_{tc_i}")
                S.add("tensor", lambda e, rr_=rr_, w=w: e.matmul(
                    pb[7][:, 0:w], onesrt.ap()[:], rr_,
                    start=True, stop=True),
                    reads=["r2", "onesrt"], writes=["pb7"],
                    tag=f"p3b{r}_{hs}_{tc_i}")
                S.add("vector", lambda e, hs=hs, c0=c0, w=w, yt=yt:
                      e.scalar_tensor_tensor(
                          out=bfa(hs, c0, w), in0=yt,
                          scalar=smv.ap()[:, 2:3], in1=pb[7][:, 0:w],
                          op0=ALU.mult, op1=ALU.mult),
                      reads=[yr, "smv_n", "pb7"], writes=[bfr(hs, c0)],
                      tag=f"p3n{r}_{hs}_{tc_i}")

        def epi(oc, pieces):
            for (j, c0, w, ps, psres) in pieces:
                xt = stg.ap()[:, j % 2, 0:w]
                xr = f"st{j % 2}"
                S.add("sync", lambda e, oc=oc, c0=c0, w=w, xt=xt, r=r:
                      e.dma_start(out=xt, in_=xfm_d[r, oc, :, c0:c0 + w]),
                      reads=[xr], writes=[xr], dma=True,
                      tag=f"wox{r}_{oc}_{j}")
                ot = stg.ap()[:, 2 + j % 2, 0:w]
                our = f"st{2 + j % 2}"
                S.add("vector", lambda e, ps=ps, xt=xt, ot=ot:
                      e.tensor_tensor(out=ot, in0=ps, in1=xt, op=ALU.add),
                      reads=[psres, xr, our], writes=[our],
                      tag=f"wor{r}_{oc}_{j}")
                S.add("sync", lambda e, oc=oc, c0=c0, w=w, ot=ot, r=r:
                      e.dma_start(out=x1_d[oc * 128:(oc + 1) * 128,
                                           r * Tc + c0 - 3:
                                           r * Tc + c0 - 3 + w], in_=ot),
                      reads=[our], writes=[f"x1o{r}_{oc}_{j}"], dma=True,
                      tag=f"wod{r}_{oc}_{j}")
        proj(w_d["wo"], OCN, 128, epi, False, f"wo{r}")

    def phase_ffn():
        for r in range(2):
            for oc in range(OCN):
                S.add("sync", lambda e, oc=oc, r=r: e.dma_start(
                    out=bfa(oc, 3, Tc),
                    in_=x1_d[oc * 128:(oc + 1) * 128, r * Tc:(r + 1) * Tc]),
                    reads=[f"x1o{r}_{oc}_{j}" for j in range(NTC)]
                    + [bfr(oc, c0) for (c0, _) in main_tc],
                    writes=[bfr(oc, c0) for (c0, _) in main_tc],
                    dma=True, tag=f"ldx1{r}_{oc}")
            ln_pass(f"L2{r}", False)
            for oc in range(OCN):
                S.add("sync", lambda e, oc=oc, r=r: e.dma_start(
                    out=ln2_d[oc * 128:(oc + 1) * 128,
                              r * Tc:(r + 1) * Tc],
                    in_=bfa(oc, 3, Tc)),
                    reads=[bfr(oc, c0) for (c0, _) in main_tc],
                    writes=[f"ln2o{r}_{oc}"], dma=True, tag=f"sl2{r}_{oc}")
                S.add("vector", lambda e, oc=oc, r=r: e.reduce_sum(
                    out=gsm.ap()[:, r * OCN + oc:r * OCN + oc + 1],
                    in_=bfa(oc, 3, Tc), axis=AX),
                    reads=[bfr(oc, c0) for (c0, _) in main_tc] + ["gsm"],
                    writes=["gsm"], tag=f"gm{r}_{oc}")
        S.add("vector", lambda e: e.tensor_tensor(
            out=gsm.ap()[:, 0:OCN], in0=gsm.ap()[:, 0:OCN],
            in1=gsm.ap()[:, OCN:2 * OCN], op=ALU.add),
            reads=["gsm"], writes=["gsm"], tag="gmadd")
        S.add("sync", lambda e: e.dma_start(
            out=gwts.ap()[:], in_=gwt_d[:].rearrange(
                "(oc p) ee -> p oc ee", p=128)),
            reads=["gwts"], writes=["gwts"], dma=True, tag="gwl")

        def lg(e):
            rr = None
            for oc in range(OCN):
                rr = e.matmul(pb[6][0:1, 0:6], gsm.ap()[:, oc:oc + 1],
                              gwts.ap()[:, oc, :], start=(oc == 0),
                              stop=(oc == OCN - 1))
            return rr
        S.add("tensor", lg, reads=["gsm", "gwts"], writes=["pb6"],
              tag="glog")
        gr = grow.ap()
        S.add("vector", lambda e: e.tensor_scalar_mul(
            out=gr[0:1, 0:6], in0=pb[6][0:1, 0:6], scalar1=1.0 / T),
            reads=["pb6"], writes=["gl"], tag="gsc")
        S.add("vector", lambda e: e.reduce_max(
            out=gr[0:1, 6:7], in_=gr[0:1, 0:6], axis=AX),
            reads=["gl"], writes=["gmx"], tag="gmax")
        S.add("vector", lambda e: e.tensor_scalar_mul(
            out=gr[0:1, 7:8], in0=gr[0:1, 6:7], scalar1=-1.0),
            reads=["gmx"], writes=["gnm"], tag="gneg")
        S.add("scalar", lambda e: e.activation(
            out=gr[0:1, 8:14], in_=gr[0:1, 0:6], func=AF.Exp,
            bias=gr[0:1, 7:8]),
            reads=["gl", "gnm"], writes=["gex"], tag="gexp")
        S.add("vector", lambda e: e.reduce_sum(
            out=gr[0:1, 14:15], in_=gr[0:1, 8:14], axis=AX),
            reads=["gex"], writes=["gsum"], tag="gsum")
        S.add("vector", lambda e: e.reciprocal(
            out=gr[0:1, 15:16], in_=gr[0:1, 14:15]),
            reads=["gsum"], writes=["grc"], tag="grec")
        S.add("vector", lambda e: e.tensor_scalar(
            out=gr[0:1, 16:22], in0=gr[0:1, 8:14], scalar1=gr[0:1, 15:16],
            scalar2=None, op0=ALU.mult),
            reads=["gex", "grc"], writes=["gp"], tag="gprob")
        S.add("vector", lambda e: e.reduce_max(
            out=gr[0:1, 22:23], in_=gr[0:1, 16:22], axis=AX),
            reads=["gp"], writes=["gm1"], tag="gm1")
        S.add("vector", lambda e: e.tensor_scalar(
            out=growu.ap()[0:1, 0:6], in0=gr[0:1, 16:22],
            scalar1=gr[0:1, 22:23], scalar2=None, op0=ALU.is_equal),
            reads=["gp", "gm1"], writes=["geq"], tag="geq")
        S.add("vector", lambda e: e.scalar_tensor_tensor(
            out=gr[0:1, 24:30], in0=growu.ap()[0:1, 0:6], scalar=-1e30,
            in1=gr[0:1, 16:22], op0=ALU.mult, op1=ALU.add),
            reads=["geq", "gp"], writes=["gpm"], tag="gpm")
        S.add("vector", lambda e: e.reduce_max(
            out=gr[0:1, 23:24], in_=gr[0:1, 24:30], axis=AX),
            reads=["gpm"], writes=["gm2"], tag="gm2")
        S.add("vector", lambda e: e.tensor_tensor(
            out=gr[0:1, 2:8], in0=growu.ap()[0:1, 0:6],
            in1=iota6t.ap()[0:1, 0:6], op=ALU.mult),
            reads=["geq", "iota6"], writes=["gia"], tag="gia")
        S.add("vector", lambda e: e.reduce_sum(
            out=gr[0:1, 0:1], in_=gr[0:1, 2:8], axis=AX),
            reads=["gia", "gl"], writes=["gib"], tag="gib")
        S.add("vector", lambda e: e.tensor_copy(
            out=gidx.ap()[0:1, 0:1], in_=gr[0:1, 0:1]),
            reads=["gib"], writes=["gi1"], tag="gi1")
        S.add("vector", lambda e: e.tensor_scalar(
            out=growu.ap()[0:1, 0:6], in0=gr[0:1, 24:30],
            scalar1=gr[0:1, 23:24], scalar2=None, op0=ALU.is_equal),
            reads=["gpm", "gm2", "geq"], writes=["geq"], tag="geq2")
        S.add("vector", lambda e: e.tensor_tensor(
            out=gr[0:1, 2:8], in0=growu.ap()[0:1, 0:6],
            in1=iota6t.ap()[0:1, 0:6], op=ALU.mult),
            reads=["geq", "iota6", "gia"], writes=["gia"], tag="gia2")
        S.add("vector", lambda e: e.reduce_sum(
            out=gr[0:1, 1:2], in_=gr[0:1, 2:8], axis=AX),
            reads=["gia"], writes=["gic"], tag="gic")
        S.add("vector", lambda e: e.tensor_copy(
            out=gidx.ap()[0:1, 8:9], in_=gr[0:1, 1:2]),
            reads=["gic"], writes=["gi2"], tag="gi2")
        S.add("vector", lambda e: e.tensor_tensor(
            out=gr[0:1, 30:31], in0=gr[0:1, 22:23], in1=gr[0:1, 23:24],
            op=ALU.add),
            reads=["gm1", "gm2"], writes=["gden"], tag="gden")
        S.add("vector", lambda e: e.reciprocal(
            out=gr[0:1, 31:32], in_=gr[0:1, 30:31]),
            reads=["gden"], writes=["gdr"], tag="gdr")
        S.add("vector", lambda e: e.tensor_tensor(
            out=gr[0:1, 26:27], in0=gr[0:1, 22:23], in1=gr[0:1, 31:32],
            op=ALU.mult),
            reads=["gm1", "gdr", "gpm"], writes=["gw1"], tag="gw1")
        S.add("vector", lambda e: e.tensor_tensor(
            out=gr[0:1, 27:28], in0=gr[0:1, 23:24], in1=gr[0:1, 31:32],
            op=ALU.mult),
            reads=["gm2", "gdr"], writes=["gw2"], tag="gw2")
        S.add("tensor", lambda e: e.matmul(
            pb[6][:, 8:9], onesrt.ap()[:], gr[0:1, 26:27],
            start=True, stop=True),
            reads=["gw1", "onesrt"], writes=["pb6"], tag="gbc1")
        S.add("tensor", lambda e: e.matmul(
            pb[6][:, 9:10], onesrt.ap()[:], gr[0:1, 27:28],
            start=True, stop=True),
            reads=["gw2", "onesrt"], writes=["pb6"], tag="gbc2")
        S.add("vector", lambda e: e.tensor_copy(
            out=smv.ap()[:, 4:6], in_=pb[6][:, 8:10]),
            reads=["pb6"], writes=["wexp"], tag="gbce")

        for s_i in range(2):
            def eb(e, s_i=s_i):
                reg = e.alloc_register(f"eb{s_i}")
                e.load(reg, gidx.ap()[0:1, 8 * s_i:8 * s_i + 1])
                rb2 = e.alloc_register(f"eb2{s_i}")
                e.reg_mul(rb2, reg, 128 * OCN)
                e.reg_mul(reg, reg, 128 * FFC)
                i1 = e.dma_start(out=b1c.ap()[:, s_i, :], in_=bass.AP(
                    b1s_d.tensor, reg, [[FFC, 128], [1, FFC]]))
                i2 = e.dma_start(out=b2c.ap()[:, s_i, :], in_=bass.AP(
                    b2s_d.tensor, rb2, [[OCN, 128], [1, OCN]]))
                e.free_register(rb2)
                e.free_register(reg)
                return [i1, i2]
            S.add("sync", eb, reads=[f"gi{s_i + 1}", "b1c", "b2c"],
                  writes=["b1c", "b2c"], inc=32, tag=f"e{s_i}b")

            def ew(e, s_i=s_i):
                reg = e.alloc_register(f"ew{s_i}")
                e.load(reg, gidx.ap()[0:1, 8 * s_i:8 * s_i + 1])
                e.reg_mul(reg, reg, D * FF)
                i1 = e.dma_start(
                    out=w1c_d[s_i * D:(s_i + 1) * D, :],
                    in_=bass.AP(w1_d.tensor, reg, [[FF, D], [1, FF]]))
                i2 = e.dma_start(
                    out=w2c_d[s_i * FF:(s_i + 1) * FF, :],
                    in_=bass.AP(w2_d.tensor, reg, [[D, FF], [1, D]]))
                e.free_register(reg)
                return [i1, i2]
            S.add("sync", ew, reads=[f"gi{s_i + 1}", "w1c", "w2c"],
                  writes=["w1c", "w2c"], inc=32, tag=f"e{s_i}w")

        hbuf = scr2.ap()[:, 0:FFC * TCW]
        for r in range(2):
            for oc in range(OCN):
                S.add("sync", lambda e, oc=oc, r=r: e.dma_start(
                    out=bfa(oc, 3, Tc),
                    in_=ln2_d[oc * 128:(oc + 1) * 128,
                              r * Tc:(r + 1) * Tc]),
                    reads=[f"ln2o{r}_{oc}"]
                    + [bfr(oc, c0) for (c0, _) in main_tc],
                    writes=[bfr(oc, c0) for (c0, _) in main_tc],
                    dma=True, tag=f"rl2{r}_{oc}")
            for s_i in range(2):
                er = f"e{r}_{s_i}"
                for tc_i, (c0, w) in enumerate(main_tc):
                    for kf in range(FFC):
                        dbi = kf % 2
                        wres = f"wc{dbi}"
                        S.add("sync", lambda e, kf=kf, dbi=dbi, s_i=s_i:
                              e.dma_start(
                                  out=wcol.ap()[:, dbi, :, :],
                                  in_=w1c_d[s_i * D:(s_i + 1) * D,
                                            kf * 128:(kf + 1) * 128]
                                  .rearrange("(kc p) m -> p kc m", p=128)),
                              reads=["w1c", wres], writes=[wres],
                              dma=True, tag=f"{er}w1_{kf}_{tc_i}")
                        bank = kf % 2

                        def mmh(dbi=dbi, c0=c0, w=w, bank=bank):
                            def f(e):
                                rr = None
                                for kc in range(KCN):
                                    rr = e.matmul(
                                        pb[bank][:, 0:w],
                                        wcol.ap()[:, dbi, kc, :],
                                        bfa(kc, c0, w),
                                        start=(kc == 0),
                                        stop=(kc == KCN - 1))
                                return rr
                            return f
                        S.add("tensor", mmh(),
                              reads=[wres] + [bfr(kc, c0)
                                              for kc in range(KCN)],
                              writes=[f"pb{bank}"],
                              tag=f"{er}hm{kf}_{tc_i}")
                        S.add("scalar", lambda e, kf=kf, w=w, bank=bank,
                              s_i=s_i: e.activation(
                                  out=hbuf[:, kf * TCW:kf * TCW + w],
                                  in_=pb[bank][:, 0:w], func=AF.Gelu,
                                  bias=b1c.ap()[:, s_i, kf:kf + 1]),
                              reads=[f"pb{bank}", "b1c", f"hb{kf}"],
                              writes=[f"hb{kf}"], tag=f"{er}hg{kf}_{tc_i}")
                    for oc in range(OCN):
                        dbi = oc % 2
                        wres = f"wc{dbi}"
                        S.add("sync", lambda e, oc=oc, dbi=dbi, s_i=s_i:
                              e.dma_start(
                                  out=wcol.ap()[:, dbi, 0:FFC, :],
                                  in_=w2c_d[s_i * FF:(s_i + 1) * FF,
                                            oc * 128:(oc + 1) * 128]
                                  .rearrange("(kf p) m -> p kf m", p=128)),
                              reads=["w2c", wres], writes=[wres],
                              dma=True, tag=f"{er}w2_{oc}_{tc_i}")
                        bank = 2 + oc % 2

                        def mmo(dbi=dbi, w=w, bank=bank):
                            def f(e):
                                rr = None
                                for kf in range(FFC):
                                    rr = e.matmul(
                                        pb[bank][:, 0:w],
                                        wcol.ap()[:, dbi, kf, :],
                                        hbuf[:, kf * TCW:kf * TCW + w],
                                        start=(kf == 0),
                                        stop=(kf == FFC - 1))
                                return rr
                            return f
                        S.add("tensor", mmo(),
                              reads=[wres] + [f"hb{kf}"
                                              for kf in range(FFC)],
                              writes=[f"pb{bank}"],
                              tag=f"{er}om{oc}_{tc_i}")
                        ot = stg.ap()[:, oc % 2, 0:w]
                        our = f"st{oc % 2}"
                        S.add("vector", lambda e, oc=oc, w=w, bank=bank,
                              ot=ot, s_i=s_i: e.tensor_scalar(
                                  out=ot, in0=pb[bank][:, 0:w],
                                  scalar1=b2c.ap()[:, s_i, oc:oc + 1],
                                  scalar2=smv.ap()[:, 4 + s_i:5 + s_i],
                                  op0=ALU.add, op1=ALU.mult),
                              reads=[f"pb{bank}", "b2c", "wexp", our],
                              writes=[our], tag=f"{er}ev{oc}_{tc_i}")
                        if s_i == 0:
                            S.add("sync", lambda e, oc=oc, c0=c0, w=w,
                                  ot=ot, r=r: e.dma_start(
                                      out=f0_d[oc * 128:(oc + 1) * 128,
                                               r * Tc + c0 - 3:
                                               r * Tc + c0 - 3 + w],
                                      in_=ot),
                                  reads=[our], writes=[f"f0o{r}_{oc}_{tc_i}"],
                                  dma=True, tag=f"{er}d{oc}_{tc_i}")
                        else:
                            f0t = stg.ap()[:, 2 + oc % 2, 0:w]
                            f0r = f"st{2 + oc % 2}"
                            S.add("sync", lambda e, oc=oc, c0=c0, w=w,
                                  f0t=f0t, r=r: e.dma_start(
                                      out=f0t,
                                      in_=f0_d[oc * 128:(oc + 1) * 128,
                                               r * Tc + c0 - 3:
                                               r * Tc + c0 - 3 + w]),
                                  reads=[f"f0o{r}_{oc}_{tc_i}", f0r],
                                  writes=[f0r], dma=True,
                                  tag=f"{er}l{oc}_{tc_i}")
                            x1t = stg.ap()[:, 4 + oc % 2, 0:w]
                            x1r = f"st{4 + oc % 2}"
                            S.add("sync", lambda e, oc=oc, c0=c0, w=w,
                                  x1t=x1t, r=r: e.dma_start(
                                      out=x1t,
                                      in_=x1_d[oc * 128:(oc + 1) * 128,
                                               r * Tc + c0 - 3:
                                               r * Tc + c0 - 3 + w]),
                                  reads=[x1r]
                                  + [f"x1o{r}_{oc}_{jj}"
                                     for jj in range(NTC)],
                                  writes=[x1r], dma=True,
                                  tag=f"{er}x{oc}_{tc_i}")
                            S.add("vector", lambda e, ot=ot, f0t=f0t:
                                  e.tensor_tensor(out=ot, in0=ot, in1=f0t,
                                                  op=ALU.add),
                                  reads=[our, f0r], writes=[our],
                                  tag=f"{er}c1{oc}_{tc_i}")
                            S.add("vector", lambda e, ot=ot, x1t=x1t:
                                  e.tensor_tensor(out=ot, in0=ot, in1=x1t,
                                                  op=ALU.add),
                                  reads=[our, x1r], writes=[our],
                                  tag=f"{er}c2{oc}_{tc_i}")
                            S.add("sync", lambda e, oc=oc, c0=c0, w=w,
                                  ot=ot, r=r: e.dma_start(
                                      out=out_d[oc * 128:(oc + 1) * 128,
                                                r * Tc + c0 - 3:
                                                r * Tc + c0 - 3 + w],
                                      in_=ot),
                                  reads=[our], writes=[f"oo{r}_{oc}_{tc_i}"],
                                  dma=True, tag=f"{er}z{oc}_{tc_i}")

    load_consts()
    for r in range(2):
        if "ln1" in phases:
            phase_ln1(r)
        S.barrier()
        if "proj" in phases:
            base = r * XGR
            proj(w_d["wq"], OCN, 128,
                 conv_epi(xgx_d[base:base + D, :], "q", f"q{r}"), True,
                 f"q{r}")
            S.barrier()
            proj(w_d["wk"], OCN, 128,
                 conv_epi(xgx_d[base + D:base + 2 * D, :], "k", f"k{r}"),
                 True, f"k{r}")
            S.barrier()
            proj(w_d["wv"], OCN, 128,
                 conv_epi(xgx_d[base + 2 * D:base + 3 * D, :], "v", f"v{r}"),
                 True, f"v{r}")
            S.barrier()
            phase_gbg(r)
            S.barrier()
            phase_gate(r)
        S.barrier()
    S.barrier()
    if "scan" in phases:
        phase_scan()
    S.barrier()
    if "p3" in phases:
        for r in range(2):
            phase_p3(r)
    S.barrier()
    if "ffn" in phases:
        phase_ffn()

    semh = {}
    for e in ENGS:
        semh[e] = nc.semaphore(f"sm_{e}").__enter__()
        for k in range(NSLOT.get(e, 0)):
            semh[(e, k)] = nc.semaphore(f"sm_{e}{k}").__enter__()
    streams = S.plan()
    blk = nc.Block()
    blk.__enter__()

    def runner(ename):
        def run(eng):
            for item in streams[ename]:
                if item[0] == "w":
                    eng.wait_ge(semh[item[1]], item[2])
                else:
                    op = item[1]
                    r = op.fn(eng)
                    if op.need_inc or op.dma:
                        sem = semh[(op.eng, op.slot) if op.dma else op.eng]
                        insts = r if isinstance(r, (list, tuple)) else [r]
                        insts = [i for i in insts if i is not None]
                        if op.dma:
                            for i in insts:
                                i.then_inc(sem, 16)
                        else:
                            insts[-1].then_inc(sem, op.inc)
        return run
    blk.sync(runner("sync"))
    blk.tensor(runner("tensor"))
    blk.vector(runner("vector"))
    blk.scalar(runner("scalar"))
    blk.gpsimd(runner("gpsimd"))
    blk.__exit__(None, None, None)
    return nc


def kernel(**inputs):
    T = 4096
    in_maps = stage_inputs(inputs, T=T)
    nc = build(T=T)
    res = run_bass_kernel_spmd(nc, in_maps, core_ids=list(range(8)))
    Tc = T // 2
    out = np.zeros((4, T, D), np.float32)
    for core in range(8):
        b, p = core // 2, core % 2
        o = np.asarray(res.results[core]["out"])
        out[b, p * Tc:(p + 1) * Tc, :] = o[:, p * Tc:(p + 1) * Tc].T
    return out
